# version 08
# baseline (speedup 1.0000x reference)
import numpy as np
from contextlib import ExitStack

import concourse.bass as bass
import concourse.mybir as mybir
from concourse.bass_utils import run_bass_kernel_spmd

F32 = mybir.dt.float32
BF16 = mybir.dt.bfloat16
AF = mybir.ActivationFunctionType
ALU = mybir.AluOpType
AX = mybir.AxisListType

NCORES = 8
DM = 4096
NT = 1024
TH = 512
NDC = DM // 128
HID = 11008
NOC = HID // 128
NFT = 17
FPAD = NFT * 128
NETR = 4

ENGINES = ("tensor", "vector", "scalar", "sync", "gpsimd")

ARN_N = 81920
YB = 0
GB = 16384
WB = 60416
W2B = 77312
B2B = 81408
AYB = 0
HBB = 0
HTB = 8192
XFC = 16384
XFS = 24576
XSL = 32768
PPB = 34816
CIV = 16384
CRR = 20736
CRT = 43520
HFC = 24576
HFS = 32768
HXS = 40960
RPB = 45056
IPB = 53760
OCV = 62464
OSV = 66816

ARNF_N = 8192
XF = 0
SQF = 2048
ABF = 3072
VF = 4096
SIL = 5120
CRTF = 0
UF = 6144


class Sem:
    def __init__(self, ctx, nc, name):
        self.h = ctx.enter_context(nc.semaphore(name))
        self.n = 0


class Plan:
    def __init__(self):
        self.q = {k: [] for k in ENGINES}
        self.last_wait = {k: {} for k in ENGINES}

    def emit(self, eng, run=None, wait=(), inc=None):
        lw = self.last_wait[eng]
        w = []
        for (s, v) in wait:
            if v > 0 and lw.get(id(s), -1) < v:
                w.append((s.h, v))
                lw[id(s)] = v
        i = None
        if inc is not None:
            inc[0].n += inc[1]
            i = (inc[0].h, inc[1])
        w = tuple(w)

        def f(e, w=w, run=run, i=i):
            for (h, v) in w:
                e.wait_ge(h, v)
            ins = run(e) if run is not None else None
            if i is not None:
                if ins is None:
                    ins = e.nop()
                ins.then_inc(*i)
        self.q[eng].append(f)


class DStream:

    def __init__(self, ctx, nc, name, plan):
        self.s = Sem(ctx, nc, name)
        self.P = plan

    def load(self, eng, fn, wait=()):
        self.P.emit(eng, fn, wait=wait, inc=(self.s, 16))
        return (self.s, self.s.n)

    def tok(self):
        return (self.s, self.s.n)


def build_nc(stop_after=None):
    nc = bass.Bass("TRN2", target_bir_lowering=False)

    xin = nc.dram_tensor("xin", [DM, NT], F32, kind="ExternalInput").ap()
    w13 = nc.dram_tensor("w13", [NOC, 128, 66, 128], F32, kind="ExternalInput").ap()
    w2 = nc.dram_tensor("w2", [NOC + 1, 128, 32, 128], F32, kind="ExternalInput").ap()
    a1 = nc.dram_tensor("a1", [128, NDC, 16], F32, kind="ExternalInput").ap()
    a3 = nc.dram_tensor("a3", [128, NDC, 16], F32, kind="ExternalInput").ap()
    a2 = nc.dram_tensor("a2", [128, NOC, 16], F32, kind="ExternalInput").ap()
    fcd = nc.dram_tensor("fcd", [NFT, 128, NDC, 128], F32, kind="ExternalInput").ap()
    fsd = nc.dram_tensor("fsd", [NFT, 128, NDC, 128], F32, kind="ExternalInput").ap()
    cid = nc.dram_tensor("cid", [NDC, 128, NFT, 128], F32, kind="ExternalInput").ap()
    sid = nc.dram_tensor("sid", [NDC, 128, NFT, 128], F32, kind="ExternalInput").ap()
    idm = nc.dram_tensor("idm", [128, 128], F32, kind="ExternalInput").ap()

    outd = nc.dram_tensor("out", [DM, NT], F32, kind="ExternalOutput").ap()
    dbg_mode = stop_after == "ffn1"
    if dbg_mode:
        ydbg = nc.dram_tensor("ydbg", [DM, NT], F32, kind="ExternalOutput").ap()
        gdbg = nc.dram_tensor("gdbg", [2048, TH], F32, kind="ExternalOutput").ap()
        tdbg = nc.dram_tensor("tdbg", [128, 3 * TH], F32, kind="ExternalOutput").ap()
        mdbg = nc.dram_tensor("mdbg", [128, TH], F32, kind="ExternalOutput").ap()
    xs = nc.dram_tensor("xs", [DM, NT], F32).ap()
    w13c = nc.dram_tensor("w13c", [NOC, 128, 66, 128], BF16).ap()
    w2c = nc.dram_tensor("w2c", [NOC + 1, 128, 32, 128], BF16).ap()
    fcc = nc.dram_tensor("fcc", [NFT, 128, NDC, 128], BF16).ap()
    fsc = nc.dram_tensor("fsc", [NFT, 128, NDC, 128], BF16).ap()
    cic = nc.dram_tensor("cic", [NDC, 128, NFT, 128], BF16).ap()
    sic = nc.dram_tensor("sic", [NDC, 128, NFT, 128], BF16).ap()
    xre_d = nc.dram_tensor("xre_d", [NFT, 128, NT], BF16).ap()
    xim_d = nc.dram_tensor("xim_d", [NFT, 128, NT], BF16).ap()

    with ExitStack() as ctx:
        E = ctx.enter_context
        ARN = E(nc.sbuf_tensor("ARN", [128, ARN_N], BF16))
        ARNF = E(nc.sbuf_tensor("ARNF", [128, ARNF_N], F32))
        a1s = E(nc.sbuf_tensor("a1s", [128, NDC * 16], BF16))
        a3s = E(nc.sbuf_tensor("a3s", [128, NDC * 16], BF16))
        a2s = E(nc.sbuf_tensor("a2s", [128, NOC * 16], BF16))
        t1p = E(nc.sbuf_tensor("t1p", [128, TH], BF16))
        t3p = E(nc.sbuf_tensor("t3p", [128, TH], BF16))
        t2p = E(nc.sbuf_tensor("t2p", [128, TH], BF16))
        ident = E(nc.sbuf_tensor("ident", [128, 128], BF16))
        onesf = E(nc.sbuf_tensor("onesf", [128, 128], F32))
        strow = E(nc.sbuf_tensor("strow", [1, 1536], F32))
        m8f = E(nc.sbuf_tensor("m8f", [128, 8], F32))
        e8 = E(nc.sbuf_tensor("e8", [128, 8], F32))
        w8 = E(nc.sbuf_tensor("w8", [128, 8], F32))
        sc2 = E(nc.sbuf_tensor("sc2", [128, 2], F32))
        twoc = E(nc.sbuf_tensor("twoc", [128, 1], F32))

        pb = [E(nc.psum_tensor(f"pb{i}", [128, 512], F32)) for i in range(6)]
        ptb = [E(nc.psum_tensor(f"ptb{i}", [128, 1024], BF16)) for i in range(2)]

        sT = Sem(ctx, nc, "sT")
        sV = Sem(ctx, nc, "sV")
        sA = Sem(ctx, nc, "sA")
        sB = Sem(ctx, nc, "sB")

        P = Plan()

        def DS(name):
            return DStream(ctx, nc, name, P)

        ini = DS("ini")
        wbs = [DS("wb0"), DS("wb1")]
        w2s = [DS("w20"), DS("w21")]
        b2s = DS("b2s")
        fcs = [DS("fc0"), DS("fc1")]
        fss = [DS("fs0"), DS("fs1")]
        cis = [DS("ci0"), DS("ci1")]
        sis = [DS("si0"), DS("si1")]
        dbs = DS("dbs")
        GP_STREAMS = [ini, wbs[0], wbs[1], w2s[0], w2s[1], b2s,
                      fcs[0], fcs[1], fss[0], fss[1], cis[0], cis[1],
                      sis[0], sis[1], dbs]
        lns = [DS("ln0"), DS("ln1")]
        uls = DS("uls")
        xhs = DS("xhs")
        vst = [DS("vst0"), DS("vst1")]
        xsp = [DS("xsp0"), DS("xsp1")]
        ost = DS("ost")
        SY_STREAMS = [lns[0], lns[1], uls, xhs, vst[0], vst[1],
                      xsp[0], xsp[1], ost]
        wk13 = [DS("wk13a"), DS("wk13b")]
        wk2 = [DS("wk2a"), DS("wk2b")]
        fck = [DS("fcka"), DS("fckb")]
        fsk = [DS("fska"), DS("fskb")]
        cik = [DS("cika"), DS("cikb")]
        sik = [DS("sika"), DS("sikb")]
        SC_STREAMS = wk13 + wk2 + fck + fsk + cik + sik

        tev = []

        def arn(off, n):
            return ARN[:, off:off + n]

        def arnf(off, n):
            return ARNF[:, off:off + n]

        def barrier():
            P.emit("gpsimd", wait=tuple(s.tok() for s in GP_STREAMS), inc=(sB, 1))
            P.emit("sync", wait=tuple(s.tok() for s in SY_STREAMS), inc=(sB, 1))
            P.emit("tensor", wait=((sT, sT.n),), inc=(sB, 1))
            P.emit("vector", wait=((sV, sV.n),), inc=(sB, 1))
            P.emit("scalar", wait=((sA, sA.n),) + tuple(s_.tok() for s_ in SC_STREAMS),
                   inc=(sB, 1))
            tgt = sB.n
            for eng in ENGINES:
                P.emit(eng, wait=((sB, tgt),))

        def init_phase():
            ini.load("gpsimd", lambda g: g.dma_start(a1s[:], a1[:, :, :]))
            ini.load("gpsimd", lambda g: g.dma_start(a3s[:], a3[:, :, :]))
            ini.load("gpsimd", lambda g: g.dma_start(a2s[:], a2[:, :, :]))
            ini.load("gpsimd", lambda g: g.dma_start(ident[:], idm[:]))
            P.emit("vector", lambda v: v.memset(onesf[:], 1.0), inc=(sV, 1))
            P.emit("vector", lambda v: v.memset(twoc[:], 2.0), inc=(sV, 1))
            P.emit("vector", lambda v: v.memset(t1p[:], 0.0), inc=(sV, 1))
            P.emit("vector", lambda v: v.memset(t3p[:], 0.0), inc=(sV, 1))
            P.emit("vector", lambda v: v.memset(t2p[:], 0.0), inc=(sV, 1))

        def layernorm_half(src, th, ybase, pb45w=()):
            t0 = th * TH
            NG = NDC // 2
            for g_ in range(NG):
                slot = g_ % 2
                xfa = arnf(XF + slot * 1024, 1024)
                ld = lns[slot].load(
                    "sync",
                    (lambda sy, xfa=xfa, g_=g_, t0=t0:
                     sy.dma_start(xfa.rearrange("p (c j) -> p c j", c=2),
                                  src[g_ * 256:(g_ + 1) * 256, t0:t0 + TH]
                                  .rearrange("(c p) j -> p c j", c=2))),
                    wait=((sT, sT.n), (sV, sV.n)))
                for ci in range(2):
                    dc = 2 * g_ + ci
                    xc = arnf(XF + slot * 1024 + ci * 512, 512)
                    sq = arnf(SQF + (dc % 2) * 512, 512)
                    P.emit("scalar",
                           (lambda s, sq=sq, xc=xc: s.activation(sq, xc, AF.Square)),
                           wait=(ld, (sT, sT.n)),
                           inc=(sA, 1))
                    sqdone = sA.n
                    P.emit("tensor",
                           (lambda t, xc=xc, dc=dc:
                            t.matmul(pb[4][0:1, :], onesf[:, 0:1], xc,
                                     start=(dc == 0), stop=(dc == NDC - 1))),
                           wait=(ld,) + ((((sA, sA.n), (sV, sV.n)) + tuple(pb45w)) if dc == 0 else ()))
                    P.emit("tensor",
                           (lambda t, sq=sq, dc=dc:
                            t.matmul(pb[5][0:1, :], onesf[:, 0:1], sq,
                                     start=(dc == 0), stop=(dc == NDC - 1))),
                           wait=((sA, sqdone),),
                           inc=(sT, 1))
            mmdone = sT.n
            mrow = strow[0:1, 0:512]
            srow = strow[0:1, 512:1024]
            brow = strow[0:1, 1024:1536]

            def step(fn, extra=()):
                P.emit("vector", fn, wait=((sV, sV.n),) + extra, inc=(sV, 1))

            step((lambda v: v.tensor_scalar(mrow, pb[4][0:1, :], 1.0 / DM, None, ALU.mult)),
                 extra=((sT, mmdone),))
            step(lambda v: v.tensor_scalar(srow, pb[5][0:1, :], 1.0 / DM, None, ALU.mult))
            step(lambda v: v.tensor_tensor(brow, mrow, mrow, ALU.mult))
            step(lambda v: v.tensor_tensor(srow, srow, brow, ALU.subtract))
            step(lambda v: v.tensor_scalar(srow, srow, 1e-5, None, ALU.add))
            vdone = sV.n
            P.emit("scalar", (lambda s: s.activation(srow, srow, AF.Sqrt)),
                   wait=((sV, vdone),), inc=(sA, 1))
            step((lambda v: v.reciprocal(srow, srow)), extra=((sA, sA.n),))
            step(lambda v: v.tensor_tensor(brow, mrow, srow, ALU.mult))
            step(lambda v: v.tensor_scalar(brow, brow, -1.0, None, ALU.mult))
            rowsdone = sV.n
            P.emit("tensor",
                   (lambda t: t.matmul(pb[4][:, :], onesf[0:1, :], srow, start=True, stop=True)),
                   wait=((sV, rowsdone),))
            P.emit("tensor",
                   (lambda t: t.matmul(pb[5][:, :], onesf[0:1, :], brow, start=True, stop=True)),
                   inc=(sT, 1))
            bcdone = sT.n
            step((lambda v: v.tensor_copy(arnf(ABF, 512), pb[4][:, :])),
                 extra=((sT, bcdone),))
            step(lambda v: v.tensor_copy(arnf(ABF + 512, 512), pb[5][:, :]))
            abdone = sV.n
            for g_ in range(NG):
                slot = g_ % 2
                xfa = arnf(XF + slot * 1024, 1024)
                ld = lns[slot].load(
                    "sync",
                    (lambda sy, xfa=xfa, g_=g_, t0=t0:
                     sy.dma_start(xfa.rearrange("p (c j) -> p c j", c=2),
                                  src[g_ * 256:(g_ + 1) * 256, t0:t0 + TH]
                                  .rearrange("(c p) j -> p c j", c=2))),
                    wait=((sV, sV.n if g_ >= 2 else abdone),))
                for ci in range(2):
                    dc = 2 * g_ + ci
                    xc = arnf(XF + slot * 1024 + ci * 512, 512)
                    tmp = arnf(SQF + (dc % 2) * 512, 512)
                    yc = arn(ybase + dc * 512, 512)
                    step((lambda v, tmp=tmp, xc=xc:
                          v.tensor_tensor(tmp, xc, arnf(ABF, 512), ALU.mult)),
                         extra=(ld,))
                    step(lambda v, yc=yc, tmp=tmp:
                         v.tensor_tensor(yc, tmp, arnf(ABF + 512, 512), ALU.add))
            return sV.n

        def ffn_half_front(src, th, ydone, dbg, wmode, pbw=None, bandfree=None):
            if pbw is None:
                pbw = (sV, 0)
            if bandfree is None:
                bandfree = []
            """lora-down + M1/M3 (g build) + t2 for one half. Returns marks."""
            t0 = th * TH
            if dbg:
                for dc in range(NDC):
                    dbs.load("gpsimd",
                             (lambda g, dc=dc, t0=t0:
                              g.dma_start(ydbg[dc * 128:dc * 128 + 128, t0:t0 + TH],
                                          arn(YB + dc * 512, 512))),
                             wait=((sV, ydone),))
            for dc in range(NDC):
                yc = arn(YB + dc * 512, 512)
                P.emit("tensor",
                       (lambda t, dc=dc, yc=yc:
                        t.matmul(pb[4][0:16, :], a1s[:, dc * 16:dc * 16 + 16], yc,
                                 start=(dc == 0), stop=(dc == NDC - 1))),
                       wait=((sV, ydone), ini.tok()))
                P.emit("tensor",
                       (lambda t, dc=dc, yc=yc:
                        t.matmul(pb[5][0:16, :], a3s[:, dc * 16:dc * 16 + 16], yc,
                                 start=(dc == 0), stop=(dc == NDC - 1))),
                       inc=(sT, 1) if dc == NDC - 1 else None)
            tl = sT.n
            P.emit("scalar",
                   (lambda s: s.activation(t1p[0:16, :], pb[4][0:16, :], AF.Copy,
                                           scale=twoc[0:16, 0:1])),
                   wait=((sT, tl),), inc=(sA, 1))
            P.emit("scalar",
                   (lambda s: s.activation(t3p[0:16, :], pb[5][0:16, :], AF.Copy,
                                           scale=twoc[0:16, 0:1])),
                   inc=(sA, 1))
            tldone = sA.n

            gvl = []
            nprev = len(bandfree)
            for ot in range(NOC):
                slot = (nprev + ot) % 2
                wband = WB + slot * 8448
                wsrc = w13 if wmode == "write" else w13c
                wtok = wbs[slot].load(
                    "gpsimd",
                    (lambda g, ot=ot, wband=wband, wsrc=wsrc:
                     g.dma_start(arn(wband, 8448), wsrc[ot, :, :, :])),
                    wait=((sT, bandfree[-2] if len(bandfree) >= 2 else 0),
                          wk13[slot].tok()))
                if wmode == "write":
                    wk13[slot].load(
                        "scalar",
                        (lambda sc, ot=ot, wband=wband:
                         sc.dma_start(w13c[ot, :, :, :], arn(wband, 8448))),
                        wait=(wtok,))
                bh1 = pb[0 + (ot % 2) * 2]
                bh3 = pb[1 + (ot % 2) * 2]
                bw = (sV, gvl[ot - 2]) if ot >= 2 else pbw
                for dc in range(NDC):
                    P.emit("tensor",
                           (lambda t, bh1=bh1, wband=wband, dc=dc:
                            t.matmul(bh1[:, :], arn(wband + dc * 128, 128),
                                     arn(YB + dc * 512, 512),
                                     start=(dc == 0), stop=False)),
                           wait=(wtok,) + ((bw,) if dc == 0 else ()))
                for dc in range(NDC):
                    P.emit("tensor",
                           (lambda t, bh3=bh3, wband=wband, dc=dc:
                            t.matmul(bh3[:, :], arn(wband + (33 + dc) * 128, 128),
                                     arn(YB + dc * 512, 512),
                                     start=(dc == 0), stop=False)))
                P.emit("tensor",
                       (lambda t, bh1=bh1, wband=wband:
                        t.matmul(bh1[:, :], arn(wband + 32 * 128, 128), t1p[:, :],
                                 start=False, stop=True)),
                       wait=((sA, tldone),))
                P.emit("tensor",
                       (lambda t, bh3=bh3, wband=wband:
                        t.matmul(bh3[:, :], arn(wband + 65 * 128, 128), t3p[:, :],
                                 start=False, stop=True)),
                       inc=(sT, 1))
                bandfree.append(sT.n)
                mmd = sT.n
                sil = arnf(SIL + (ot % 2) * 512, 512)
                P.emit("scalar",
                       (lambda s, sil=sil, bh1=bh1: s.activation(sil, bh1[:, :], AF.Silu)),
                       wait=((sT, mmd),
                             (sV, gvl[ot - 2]) if ot >= 2 else pbw),
                       inc=(sA, 1))
                sd = sA.n
                P.emit("vector",
                       (lambda v, ot=ot, sil=sil, bh3=bh3:
                        v.tensor_tensor(arn(GB + ot * 512, 512), sil, bh3[:, :], ALU.mult)),
                       wait=((sA, sd),), inc=(sV, 1))
                gvl.append(sV.n)
            gdone = sV.n
            if dbg and th == 0:
                for i, ot in enumerate(list(range(8)) + list(range(78, 86))):
                    dbs.load("gpsimd",
                             (lambda g, i=i, ot=ot:
                              g.dma_start(gdbg[i * 128:i * 128 + 128, :],
                                          arn(GB + ot * 512, 512))),
                             wait=((sV, gdone),))
                dbs.load("gpsimd", (lambda g: g.dma_start(tdbg[:, 0:TH], t1p[:, :])),
                         wait=((sA, tldone),))
                dbs.load("gpsimd", (lambda g: g.dma_start(tdbg[:, TH:2 * TH], t3p[:, :])))

            for oc in range(NOC):
                P.emit("tensor",
                       (lambda t, oc=oc:
                        t.matmul(pb[4][0:16, :], a2s[:, oc * 16:oc * 16 + 16],
                                 arn(GB + oc * 512, 512),
                                 start=(oc == 0), stop=(oc == NOC - 1))),
                       wait=((sV, gdone),) if oc == 0 else (),
                       inc=(sT, 1) if oc == NOC - 1 else None)
            P.emit("scalar",
                   (lambda s: s.activation(t2p[0:16, :], pb[4][0:16, :], AF.Copy,
                                           scale=twoc[0:16, 0:1])),
                   wait=((sT, sT.n),), inc=(sA, 1))
            t2done = sA.n
            if dbg and th == 0:
                dbs.load("gpsimd", (lambda g: g.dma_start(tdbg[:, 2 * TH:3 * TH], t2p[:, :])),
                         wait=((sA, t2done),))
            return gdone, t2done, bandfree

        def ffn_half_m2(src, dst, th, gdone, t2done, wmode):
            t0 = th * TH
            sttm = []
            w2free = []
            NGW = (NOC + 7) // 8
            for rg in range(16):
                et0 = rg * 2
                pr = rg % 2
                bks = (pb[2 * pr], pb[2 * pr + 1])
                if rg % 2 == 0:
                    b2tok = b2s.load(
                        "gpsimd",
                        (lambda g, et0=et0:
                         g.dma_start(arn(B2B, 512).rearrange("p (e j) -> p e j", e=4),
                                     w2[NOC, :, et0:et0 + 4, :])),
                        wait=((sT, sT.n),))
                    utok = uls.load(
                        "sync",
                        (lambda sy, et0=et0, t0=t0:
                         sy.dma_start(arnf(UF, 2048).rearrange("p (e j) -> p e j", e=4),
                                      src[et0 * 128:et0 * 128 + 512, t0:t0 + TH]
                                      .rearrange("(e p) j -> p e j", e=4))),
                        wait=((sV, sttm[rg - 1]) if rg >= 1 else (sV, 0),))
                    b2u = (b2tok, utok)
                else:
                    b2tok, utok = b2u
                uslot = UF + (rg % 2) * 1024
                for gw in range(NGW):
                    oc0 = gw * 8
                    noc = min(8, NOC - oc0)
                    slot = len(w2free) % 2
                    w2src = w2 if wmode == "write" else w2c
                    wtok = w2s[slot].load(
                        "gpsimd",
                        (lambda g, oc0=oc0, noc=noc, slot=slot, et0=et0, w2src=w2src:
                         g.dma_start(
                             arn(W2B + slot * 2048, noc * 256)
                             .rearrange("p (o x) -> p o x", o=noc),
                             w2src[oc0:oc0 + noc, :, et0:et0 + 2, :]
                             .rearrange("o p e j -> p o (e j)"))),
                        wait=((sT, w2free[-2] if len(w2free) >= 2 else gdone * 0),
                              wk2[slot].tok()))
                    if wmode == "write":
                        wk2[slot].load(
                            "scalar",
                            (lambda sc, oc0=oc0, noc=noc, slot=slot, et0=et0:
                             sc.dma_start(
                                 w2c[oc0:oc0 + noc, :, et0:et0 + 2, :]
                                 .rearrange("o p e j -> p o (e j)"),
                                 arn(W2B + slot * 2048, noc * 256)
                                 .rearrange("p (o x) -> p o x", o=noc))),
                            wait=(wtok,))
                    for ei in range(2):
                        for oi in range(noc):
                            oc = oc0 + oi
                            last = (oi == noc - 1 and ei == 1)
                            P.emit("tensor",
                                   (lambda t, slot=slot, oi=oi, ei=ei, oc=oc, bks=bks:
                                    t.matmul(bks[ei][:, :],
                                             arn(W2B + slot * 2048 + oi * 256 + ei * 128, 128),
                                             arn(GB + oc * 512, 512),
                                             start=(oc == 0), stop=False)),
                                   wait=(wtok,) +
                                        (((sV, sttm[rg - 2] if rg >= 2 else gdone),)
                                         if (oc == 0 and ei == 0) else ()),
                                   inc=(sT, 1) if last else None)
                    w2free.append(sT.n)
                for ei in range(2):
                    P.emit("tensor",
                           (lambda t, ei=ei, rg=rg, bks=bks:
                            t.matmul(bks[ei][:, :],
                                     arn(B2B + (rg % 2) * 256 + ei * 128, 128), t2p[:, :],
                                     start=False, stop=True)),
                           wait=(b2tok, (sA, t2done)),
                           inc=(sT, 1) if ei == 1 else None)
                rgmm = sT.n
                for ei in range(2):
                    vslot = arnf(VF + (ei % 2) * 512, 512)
                    P.emit("vector",
                           (lambda v, ei=ei, vslot=vslot, bks=bks, uslot=uslot:
                            v.scalar_tensor_tensor(vslot, bks[ei][:, :], 0.5,
                                                   arnf(uslot + ei * 512, 512),
                                                   ALU.mult, ALU.add)),
                           wait=((sT, rgmm), utok, vst[ei % 2].tok()),
                           inc=(sV, 1))
                    vd = sV.n
                    vst[ei % 2].load(
                        "sync",
                        (lambda sy, ei=ei, et0=et0, vslot=vslot, t0=t0:
                         sy.dma_start(dst[(et0 + ei) * 128:(et0 + ei) * 128 + 128,
                                          t0:t0 + TH], vslot)),
                        wait=((sV, vd),))
                sttm.append(sV.n)

        def ffn_pass(src, dst, dbg=False, wmodes=("read", "read")):
            ydone0 = layernorm_half(src, 0, YB)
            g0, t2d0, bf0 = ffn_half_front(src, 0, ydone0, dbg, wmodes[0])
            ydone1 = layernorm_half(src, 1, YB, pb45w=((sA, t2d0),))
            ffn_half_m2(src, dst, 0, g0, t2d0, wmodes[0])
            m2done0 = (sV, sV.n)
            g1, t2d1, _ = ffn_half_front(src, 1, ydone1, dbg, wmodes[1],
                                         pbw=m2done0, bandfree=bf0)
            ffn_half_m2(src, dst, 1, g1, t2d1, wmodes[1])

        def attn_phase():
            for th in range(2):
                cmode = "write" if th == 0 else "read"
                t0 = th * TH
                ydone = layernorm_half(xs, th, AYB)

                xtf, xev, ppv = [], [], []
                base_t = sT.n
                for ft in range(NFT):
                    slot = ft % 2
                    fcb = XFC + slot * 4096
                    fsb = XFS + slot * 4096
                    fcsrc = fcd if cmode == "write" else fcc
                    fssrc = fsd if cmode == "write" else fsc
                    fctok = fcs[slot].load(
                        "gpsimd",
                        (lambda g, ft=ft, fcb=fcb, fcsrc=fcsrc:
                         g.dma_start(arn(fcb, 4096), fcsrc[ft, :, :, :])),
                        wait=((sT, xtf[ft - 2] if ft >= 2 else base_t),
                              fck[slot].tok()))
                    fstok = fss[slot].load(
                        "gpsimd",
                        (lambda g, ft=ft, fsb=fsb, fssrc=fssrc:
                         g.dma_start(arn(fsb, 4096), fssrc[ft, :, :, :])),
                        wait=(fsk[slot].tok(),))
                    if cmode == "write":
                        fck[slot].load(
                            "scalar",
                            (lambda sc, ft=ft, fcb=fcb:
                             sc.dma_start(fcc[ft, :, :, :], arn(fcb, 4096))),
                            wait=(fctok,))
                        fsk[slot].load(
                            "scalar",
                            (lambda sc, ft=ft, fsb=fsb:
                             sc.dma_start(fsc[ft, :, :, :], arn(fsb, 4096))),
                            wait=(fstok,))
                    bre = pb[0 + 2 * slot]
                    bim = pb[1 + 2 * slot]
                    bw = (sA, xev[ft - 2]) if ft >= 2 else (sV, 0)
                    for dc in range(NDC):
                        P.emit("tensor",
                               (lambda t, bre=bre, fcb=fcb, dc=dc:
                                t.matmul(bre[:, :], arn(fcb + dc * 128, 128),
                                         arn(AYB + dc * 512, 512),
                                         start=(dc == 0), stop=(dc == NDC - 1))),
                               wait=(fctok, (sV, ydone)) + ((bw,) if dc == 0 else ()))
                    for dc in range(NDC):
                        P.emit("tensor",
                               (lambda t, bim=bim, fsb=fsb, dc=dc:
                                t.matmul(bim[:, :], arn(fsb + dc * 128, 128),
                                         arn(AYB + dc * 512, 512),
                                         start=(dc == 0), stop=(dc == NDC - 1))),
                               wait=(fstok,) if dc == 0 else (),
                               inc=(sT, 1) if dc == NDC - 1 else None)
                    xtf.append(sT.n)
                    mmd = sT.n
                    xre_sl = arn(XSL + slot * 1024, 512)
                    xim_sl = arn(XSL + slot * 1024 + 512, 512)
                    tq1 = arnf(SIL, 512)
                    tq2 = arnf(SIL + 512, 512)
                    P.emit("scalar",
                           (lambda s, xre_sl=xre_sl, bre=bre: s.copy(xre_sl, bre[:, :])),
                           wait=((sT, mmd), xsp[slot].tok(),
                                 (sV, ppv[ft - 1] if ft >= 1 else 0)),
                           inc=(sA, 1))
                    P.emit("scalar",
                           (lambda s, xim_sl=xim_sl, bim=bim: s.copy(xim_sl, bim[:, :])),
                           inc=(sA, 1))
                    P.emit("scalar",
                           (lambda s, tq1=tq1, bre=bre: s.activation(tq1, bre[:, :], AF.Square)),
                           inc=(sA, 1))
                    P.emit("scalar",
                           (lambda s, tq2=tq2, bim=bim: s.activation(tq2, bim[:, :], AF.Square)),
                           inc=(sA, 1))
                    xev.append(sA.n)
                    evd = sA.n
                    P.emit("vector",
                           (lambda v, ft=ft, tq1=tq1, tq2=tq2:
                            v.tensor_tensor(arn(PPB + ft * 512, 512), tq1, tq2, ALU.add)),
                           wait=((sA, evd),), inc=(sV, 1))
                    ppv.append(sV.n)
                    xsp[slot].load(
                        "sync",
                        (lambda sy, ft=ft, xre_sl=xre_sl, t0=t0:
                         sy.dma_start(xre_d[ft, :, t0:t0 + TH], xre_sl)),
                        wait=((sA, evd),))
                    xsp[slot].load(
                        "sync",
                        (lambda sy, ft=ft, xim_sl=xim_sl, t0=t0:
                         sy.dma_start(xim_d[ft, :, t0:t0 + TH], xim_sl)))
                barrier()

                ctf, crevl, trlast = [], [], []
                base_t = sT.n
                pend = None
                for lt in range(NDC):
                    slot = lt % 2
                    civ = CIV + slot * 2176
                    cisrc = cid if cmode == "write" else cic
                    citok = cis[slot].load(
                        "gpsimd",
                        (lambda g, lt=lt, civ=civ, cisrc=cisrc:
                         g.dma_start(arn(civ, 2176), cisrc[lt, :, :, :])),
                        wait=((sT, ctf[lt - 2] if lt >= 2 else base_t),
                              cik[slot].tok()))
                    if cmode == "write":
                        cik[slot].load(
                            "scalar",
                            (lambda sc, lt=lt, civ=civ:
                             sc.dma_start(cic[lt, :, :, :], arn(civ, 2176))),
                            wait=(citok,))
                    bco = pb[slot]
                    for kc in range(NFT):
                        P.emit("tensor",
                               (lambda t, bco=bco, civ=civ, kc=kc:
                                t.matmul(bco[:, :], arn(civ + kc * 128, 128),
                                         arn(PPB + kc * 512, 512),
                                         start=(kc == 0), stop=(kc == NFT - 1))),
                               wait=(citok,) +
                                    (((sA, crevl[lt - 2]),) if (kc == 0 and lt >= 2) else ()),
                               inc=(sT, 1) if kc == NFT - 1 else None)
                    ctf.append(sT.n)
                    cmm = sT.n

                    def emit_transposes(crs, plt, crd):
                        for tt in range(4):
                            pt = ptb[len(tev) % 2]
                            P.emit("tensor",
                                   (lambda t, pt=pt, crs=crs, tt=tt:
                                    t.transpose(pt[:, 0:128], crs[:, tt * 128:tt * 128 + 128],
                                                ident[:])),
                                   wait=((sA, crd), (sA, tev[-2] if len(tev) >= 2 else 0)),
                                   inc=(sT, 1))
                            td = sT.n
                            P.emit("scalar",
                                   (lambda s, tt=tt, plt=plt, pt=pt:
                                    s.copy(arn(CRT + tt * 4096 + plt * 128, 128), pt[:, 0:128])),
                                   wait=((sT, td),), inc=(sA, 1))
                            tev.append(sA.n)
                        trlast.append(sT.n)

                    crs = arn(CRR + slot * 512, 512)
                    P.emit("scalar",
                           (lambda s, crs=crs, bco=bco: s.copy(crs, bco[:, :])),
                           wait=((sT, cmm), (sT, trlast[lt - 2] if lt >= 2 else 0)),
                           inc=(sA, 1))
                    crevl.append(sA.n)
                    crd = sA.n
                    if pend is not None:
                        emit_transposes(*pend)
                    pend = (crs, lt, crd)
                emit_transposes(*pend)
                barrier()

                for tt in range(4):
                    ctile = arn(CRT + tt * 4096, 4096)
                    cf = arnf(CRTF, 4096)

                    def vstep(fn, extra=()):
                        P.emit("vector", fn, wait=((sV, sV.n),) + extra, inc=(sV, 1))

                    vstep((lambda v, ctile=ctile, cf=cf: v.tensor_copy(cf, ctile)),
                          extra=((sT, sT.n), (sA, sA.n)))
                    vstep(lambda v, cf=cf: v.max(out=m8f[:], in_=cf))
                    vstep(lambda v: v.tensor_scalar(sc2[:, 0:1], m8f[:, 0:1], -1.0,
                                                    None, ALU.mult))
                    ng = sV.n
                    P.emit("scalar",
                           (lambda s: s.activation(e8[:], m8f[:], AF.Exp, bias=sc2[:, 0:1])),
                           wait=((sV, ng),), inc=(sA, 1))
                    vstep((lambda v: v.reduce_sum(sc2[:, 1:2], e8[:], axis=AX.X)),
                          extra=((sA, sA.n),))
                    vstep(lambda v: v.reciprocal(sc2[:, 1:2], sc2[:, 1:2]))
                    vstep(lambda v: v.tensor_scalar(w8[:], e8[:], sc2[:, 1:2], None, ALU.mult))
                    hacc = arn(HBB, 4096)
                    hk = arn(HBB + 4096, 4096)
                    vstep(lambda v, ctile=ctile, hacc=hacc:
                          v.tensor_scalar(hacc, ctile, m8f[:, 0:1], w8[:, 0:1],
                                          ALU.is_equal, ALU.mult))
                    for k in range(1, 8):
                        vstep(lambda v, ctile=ctile, hk=hk, k=k:
                              v.tensor_scalar(hk, ctile, m8f[:, k:k + 1], w8[:, k:k + 1],
                                              ALU.is_equal, ALU.mult))
                        vstep(lambda v, hacc=hacc, hk=hk:
                              v.tensor_tensor(hacc, hacc, hk, ALU.add))
                    hd = sV.n
                    for dc in range(NDC):
                        pt = ptb[len(tev) % 2]
                        P.emit("tensor",
                               (lambda t, pt=pt, hacc=hacc, dc=dc:
                                t.transpose(pt[:, 0:128], hacc[:, dc * 128:dc * 128 + 128],
                                            ident[:])),
                               wait=((sV, hd), (sA, tev[-2] if len(tev) >= 2 else 0)),
                               inc=(sT, 1))
                        td = sT.n
                        P.emit("scalar",
                               (lambda s, dc=dc, tt=tt, pt=pt:
                                s.copy(arn(HTB + dc * 512 + tt * 128, 128), pt[:, 0:128])),
                               wait=((sT, td),), inc=(sA, 1))
                        tev.append(sA.n)
                barrier()

                htf, hev, riv = [], [], []
                base_t = sT.n
                for ft in range(NFT):
                    slot = ft % 2
                    fcb = HFC + slot * 4096
                    fsb = HFS + slot * 4096
                    fctok = fcs[slot].load(
                        "gpsimd",
                        (lambda g, ft=ft, fcb=fcb:
                         g.dma_start(arn(fcb, 4096), fcc[ft, :, :, :])),
                        wait=((sT, htf[ft - 2] if ft >= 2 else base_t),))
                    fstok = fss[slot].load(
                        "gpsimd",
                        (lambda g, ft=ft, fsb=fsb:
                         g.dma_start(arn(fsb, 4096), fsc[ft, :, :, :])))
                    xrs = arn(HXS, 512)
                    xis = arn(HXS + 512, 512)
                    xhs.load("sync",
                             (lambda sy, ft=ft, xrs=xrs, t0=t0:
                              sy.dma_start(xrs, xre_d[ft, :, t0:t0 + TH])),
                             wait=((sV, riv[ft - 1] if ft >= 1 else 0),))
                    xhtok = xhs.load(
                        "sync",
                        (lambda sy, ft=ft, xis=xis, t0=t0:
                         sy.dma_start(xis, xim_d[ft, :, t0:t0 + TH])))
                    bre = pb[0 + 2 * slot]
                    bim = pb[1 + 2 * slot]
                    bw = (sA, hev[ft - 2]) if ft >= 2 else (sV, 0)
                    for lc in range(NDC):
                        P.emit("tensor",
                               (lambda t, bre=bre, fcb=fcb, lc=lc:
                                t.matmul(bre[:, :], arn(fcb + lc * 128, 128),
                                         arn(HTB + lc * 512, 512),
                                         start=(lc == 0), stop=(lc == NDC - 1))),
                               wait=(fctok, (sA, tev[-1] if tev else 0)) +
                                    ((bw,) if lc == 0 else ()))
                    for lc in range(NDC):
                        P.emit("tensor",
                               (lambda t, bim=bim, fsb=fsb, lc=lc:
                                t.matmul(bim[:, :], arn(fsb + lc * 128, 128),
                                         arn(HTB + lc * 512, 512),
                                         start=(lc == 0), stop=(lc == NDC - 1))),
                               wait=(fstok,) if lc == 0 else (),
                               inc=(sT, 1) if lc == NDC - 1 else None)
                    htf.append(sT.n)
                    mmd = sT.n
                    hre = arn(HXS + 1024, 512)
                    him = arn(HXS + 1536, 512)
                    P.emit("scalar", (lambda s, hre=hre, bre=bre: s.copy(hre, bre[:, :])),
                           wait=((sT, mmd), (sV, riv[ft - 1] if ft >= 1 else 0)),
                           inc=(sA, 1))
                    P.emit("scalar", (lambda s, him=him, bim=bim: s.copy(him, bim[:, :])),
                           inc=(sA, 1))
                    hevd = sA.n
                    hev.append(hevd)
                    tm = [arn(HXS + 2048 + i * 512, 512) for i in range(4)]

                    def vstep2(fn, extra=()):
                        P.emit("vector", fn, wait=((sV, sV.n),) + extra, inc=(sV, 1))

                    vstep2((lambda v, tm=tm, xrs=xrs, hre=hre:
                            v.tensor_tensor(tm[0], xrs, hre, ALU.mult)),
                           extra=((sA, hevd), xhtok))
                    vstep2(lambda v, tm=tm, xis=xis, him=him:
                           v.tensor_tensor(tm[1], xis, him, ALU.mult))
                    vstep2(lambda v, tm=tm, xis=xis, hre=hre:
                           v.tensor_tensor(tm[2], xis, hre, ALU.mult))
                    vstep2(lambda v, tm=tm, xrs=xrs, him=him:
                           v.tensor_tensor(tm[3], xrs, him, ALU.mult))
                    vstep2(lambda v, tm=tm, ft=ft:
                           v.tensor_tensor(arn(RPB + ft * 512, 512), tm[0], tm[1], ALU.add))
                    vstep2(lambda v, tm=tm, ft=ft:
                           v.tensor_tensor(arn(IPB + ft * 512, 512), tm[2], tm[3], ALU.subtract))
                    riv.append(sV.n)
                barrier()

                otf, ovv = [], []
                base_t = sT.n
                for dt in range(NDC):
                    slot = dt % 2
                    civ = OCV + slot * 2176
                    siv = OSV + slot * 2176
                    citok = cis[slot].load(
                        "gpsimd",
                        (lambda g, dt=dt, civ=civ:
                         g.dma_start(arn(civ, 2176), cic[dt, :, :, :])),
                        wait=((sT, otf[dt - 2] if dt >= 2 else base_t),))
                    sisrc = sid if cmode == "write" else sic
                    sitok = sis[slot].load(
                        "gpsimd",
                        (lambda g, dt=dt, siv=siv, sisrc=sisrc:
                         g.dma_start(arn(siv, 2176), sisrc[dt, :, :, :])),
                        wait=(sik[slot].tok(),))
                    if cmode == "write":
                        sik[slot].load(
                            "scalar",
                            (lambda sc, dt=dt, siv=siv:
                             sc.dma_start(sic[dt, :, :, :], arn(siv, 2176))),
                            wait=(sitok,))
                    utok = uls.load(
                        "sync",
                        (lambda sy, dt=dt, t0=t0:
                         sy.dma_start(arnf(UF, 512),
                                      xs[dt * 128:dt * 128 + 128, t0:t0 + TH])),
                        wait=((sV, ovv[dt - 1] if dt >= 1 else 0),))
                    bo = pb[slot]
                    for kc in range(NFT):
                        P.emit("tensor",
                               (lambda t, bo=bo, civ=civ, kc=kc:
                                t.matmul(bo[:, :], arn(civ + kc * 128, 128),
                                         arn(RPB + kc * 512, 512),
                                         start=(kc == 0), stop=False)),
                               wait=(citok,) +
                                    (((sV, ovv[dt - 2]),) if (kc == 0 and dt >= 2) else ()))
                    for kc in range(NFT):
                        P.emit("tensor",
                               (lambda t, bo=bo, siv=siv, kc=kc:
                                t.matmul(bo[:, :], arn(siv + kc * 128, 128),
                                         arn(IPB + kc * 512, 512),
                                         start=False, stop=(kc == NFT - 1))),
                               wait=(sitok,) if kc == 0 else (),
                               inc=(sT, 1) if kc == NFT - 1 else None)
                    otf.append(sT.n)
                    omm = sT.n
                    vslot = arnf(VF + (dt % 2) * 512, 512)
                    P.emit("vector",
                           (lambda v, vslot=vslot, bo=bo:
                            v.scalar_tensor_tensor(vslot, bo[:, :], 1.0, arnf(UF, 512),
                                                   ALU.mult, ALU.add)),
                           wait=((sT, omm), utok, vst[dt % 2].tok()),
                           inc=(sV, 1))
                    ovv.append(sV.n)
                    vd = sV.n
                    vst[dt % 2].load(
                        "sync",
                        (lambda sy, dt=dt, vslot=vslot, t0=t0:
                         sy.dma_start(xs[dt * 128:dt * 128 + 128, t0:t0 + TH], vslot)),
                        wait=((sV, vd),))
                barrier()

        def finish_wait():
            P.emit("sync", wait=tuple(s.tok() for s in SY_STREAMS))

        init_phase()
        barrier()
        if stop_after == "ffn1":
            ffn_pass(xin, outd, dbg=True, wmodes=("write", "read"))
            barrier()
            finish_wait()
        else:
            ffn_pass(xin, xs, wmodes=("write", "read"))
            barrier()
            attn_phase()
            if stop_after == "attn":
                for dc in range(8):
                    ost.load("sync",
                             (lambda sy, dc=dc:
                              sy.dma_start(outd[dc * 512:(dc + 1) * 512, :],
                                           xs[dc * 512:(dc + 1) * 512, :])))
                finish_wait()
            else:
                ffn_pass(xs, outd, wmodes=("read", "read"))
                barrier()
                finish_wait()

        with nc.Block() as block:
            @block.tensor
            def _(e):
                for f in P.q["tensor"]:
                    f(e)

            @block.vector
            def _(e):
                for f in P.q["vector"]:
                    f(e)

            @block.scalar
            def _(e):
                for f in P.q["scalar"]:
                    f(e)

            @block.sync
            def _(e):
                for f in P.q["sync"]:
                    f(e)

            @block.gpsimd
            def _(e):
                for f in P.q["gpsimd"]:
                    f(e)

    return nc


_cached = {}


def _prep_host(L0_ffn_w1, L0_ffn_w3, L0_ffn_w2,
               L0_ffn_lora_A1, L0_ffn_lora_B1, L0_ffn_lora_A3, L0_ffn_lora_B3,
               L0_ffn_lora_A2, L0_ffn_lora_B2):
    def tile_w13_half(W, B):
        WT = np.ascontiguousarray(np.asarray(W, dtype=np.float32).T)
        blk = WT.reshape(NDC, 128, NOC, 128).transpose(2, 1, 0, 3)
        BT = np.ascontiguousarray(np.asarray(B, dtype=np.float32).T)
        bb = BT.reshape(16, NOC, 128).transpose(1, 0, 2)
        out = np.zeros((NOC, 128, 33, 128), dtype=np.float32)
        out[:, :, 0:32, :] = blk
        out[:, 0:16, 32, :] = bb
        return out

    b1 = tile_w13_half(L0_ffn_w1, L0_ffn_lora_B1)
    b3 = tile_w13_half(L0_ffn_w3, L0_ffn_lora_B3)
    w13 = np.ascontiguousarray(np.concatenate([b1, b3], axis=2))

    W2T = np.ascontiguousarray(np.asarray(L0_ffn_w2, dtype=np.float32).T)
    w2 = np.zeros((NOC + 1, 128, 32, 128), dtype=np.float32)
    w2[0:NOC] = W2T.reshape(NOC, 128, 32, 128)
    B2T = np.ascontiguousarray(np.asarray(L0_ffn_lora_B2, dtype=np.float32).T)
    w2[NOC, 0:16, :, :] = B2T.reshape(16, 32, 128)
    w2 = np.ascontiguousarray(w2)

    def tile_a(A, nb):
        AT = np.ascontiguousarray(np.asarray(A, dtype=np.float32).T)
        return np.ascontiguousarray(AT.reshape(nb, 128, 16).transpose(1, 0, 2))

    a1 = tile_a(L0_ffn_lora_A1, NDC)
    a3 = tile_a(L0_ffn_lora_A3, NDC)
    a2 = tile_a(L0_ffn_lora_A2, NOC)

    L = DM
    d = np.arange(L, dtype=np.int64)
    k = np.arange(FPAD, dtype=np.int64)
    ang = (2.0 * np.pi / L) * ((d[:, None] * k[None, :]) % L).astype(np.float64)
    fc = np.cos(ang)
    fsn = -np.sin(ang)
    fc[:, 2049:] = 0.0
    fsn[:, 2049:] = 0.0
    fcdh = np.ascontiguousarray(
        fc.astype(np.float32).reshape(NDC, 128, NFT, 128).transpose(2, 1, 0, 3))
    fsdh = np.ascontiguousarray(
        fsn.astype(np.float32).reshape(NDC, 128, NFT, 128).transpose(2, 1, 0, 3))

    alpha = np.zeros(FPAD)
    alpha[0] = 1.0
    alpha[1:2048] = 2.0
    alpha[2048] = 1.0
    angi = (2.0 * np.pi / L) * ((k[:, None] * d[None, :]) % L).astype(np.float64)
    ci = (alpha[:, None] / L) * np.cos(angi)
    si = -(alpha[:, None] / L) * np.sin(angi)
    ci[2049:, :] = 0.0
    si[2049:, :] = 0.0
    cidh = np.ascontiguousarray(
        ci.astype(np.float32).reshape(NFT, 128, NDC, 128).transpose(2, 1, 0, 3))
    sidh = np.ascontiguousarray(
        si.astype(np.float32).reshape(NFT, 128, NDC, 128).transpose(2, 1, 0, 3))

    idm = np.ascontiguousarray(np.eye(128, dtype=np.float32))
    return dict(w13=w13, w2=w2, a1=a1, a3=a3, a2=a2,
                fcd=fcdh, fsd=fsdh, cid=cidh, sid=sidh, idm=idm)


def kernel(L0_attn_x, L0_ffn_w1, L0_ffn_w3, L0_ffn_w2,
           L0_ffn_lora_A1, L0_ffn_lora_B1, L0_ffn_lora_A3, L0_ffn_lora_B3,
           L0_ffn_lora_A2, L0_ffn_lora_B2, _stop_after=None, _cores=None):
    x = np.asarray(L0_attn_x, dtype=np.float32).reshape(NCORES * NT, DM)
    shared = _prep_host(L0_ffn_w1, L0_ffn_w3, L0_ffn_w2,
                        L0_ffn_lora_A1, L0_ffn_lora_B1, L0_ffn_lora_A3,
                        L0_ffn_lora_B3, L0_ffn_lora_A2, L0_ffn_lora_B2)

    if _stop_after not in _cached:
        _cached[_stop_after] = build_nc(stop_after=_stop_after)
    nc = _cached[_stop_after]

    cores = list(range(NCORES)) if _cores is None else _cores
    in_maps = []
    for c in cores:
        xT = np.ascontiguousarray(x[c * NT:(c + 1) * NT, :].T)
        m = dict(shared)
        m["xin"] = xT
        in_maps.append(m)

    res = run_bass_kernel_spmd(nc, in_maps, core_ids=list(range(len(cores))))
    if len(cores) < NCORES:
        return [{k: np.asarray(v) for k, v in r.items()} for r in res.results]
    outs = [np.asarray(r["out"]) for r in res.results]
    full = np.concatenate([o.T for o in outs], axis=0)
    return np.ascontiguousarray(full.reshape(2, 4096, DM))


# version 09
# speedup vs baseline: 1.0357x; 1.0357x over previous
import numpy as np
from contextlib import ExitStack

import concourse.bass as bass
import concourse.mybir as mybir
from concourse.bass_utils import run_bass_kernel_spmd

F32 = mybir.dt.float32
BF16 = mybir.dt.bfloat16
AF = mybir.ActivationFunctionType
ALU = mybir.AluOpType
AX = mybir.AxisListType

NCORES = 8
DM = 4096
NT = 1024
TH = 512
NDC = DM // 128
HID = 11008
NOC = HID // 128
NFT = 17
FPAD = NFT * 128
NETR = 4

ENGINES = ("tensor", "vector", "scalar", "sync", "gpsimd")

ARN_N = 81920
YB = 0
GB = 16384
WB = 60416
W2B = 77312
B2B = 81408
AYB = 0
HBB = 0
HTB = 8192
XFC = 16384
XFS = 24576
XSL = 32768
PPB = 34816
CIV = 16384
CRR = 20736
CRT = 43520
HFC = 24576
HFS = 32768
HXS = 40960
RPB = 45056
IPB = 53760
OCV = 62464
OSV = 66816

ARNF_N = 8192
XF = 0
SQF = 2048
ABF = 3072
VF = 4096
SIL = 5120
CRTF = 0
UF = 6144


class Sem:
    def __init__(self, ctx, nc, name):
        self.h = ctx.enter_context(nc.semaphore(name))
        self.n = 0


class Plan:
    def __init__(self):
        self.q = {k: [] for k in ENGINES}
        self.last_wait = {k: {} for k in ENGINES}

    def emit(self, eng, run=None, wait=(), inc=None):
        lw = self.last_wait[eng]
        w = []
        for (s, v) in wait:
            if v > 0 and lw.get(id(s), -1) < v:
                w.append((s.h, v))
                lw[id(s)] = v
        i = None
        if inc is not None:
            inc[0].n += inc[1]
            i = (inc[0].h, inc[1])
        w = tuple(w)

        def f(e, w=w, run=run, i=i):
            for (h, v) in w:
                e.wait_ge(h, v)
            ins = run(e) if run is not None else None
            if i is not None:
                if ins is None:
                    ins = e.nop()
                ins.then_inc(*i)
        self.q[eng].append(f)


class DStream:

    def __init__(self, ctx, nc, name, plan):
        self.s = Sem(ctx, nc, name)
        self.P = plan

    def load(self, eng, fn, wait=()):
        self.P.emit(eng, fn, wait=wait, inc=(self.s, 16))
        return (self.s, self.s.n)

    def tok(self):
        return (self.s, self.s.n)


def build_nc(stop_after=None):
    nc = bass.Bass("TRN2", target_bir_lowering=False)

    xin = nc.dram_tensor("xin", [DM, NT], F32, kind="ExternalInput").ap()
    w13 = nc.dram_tensor("w13", [NOC, 128, 66, 128], F32, kind="ExternalInput").ap()
    w2 = nc.dram_tensor("w2", [NOC + 1, 128, 32, 128], F32, kind="ExternalInput").ap()
    a1 = nc.dram_tensor("a1", [128, NDC, 16], F32, kind="ExternalInput").ap()
    a3 = nc.dram_tensor("a3", [128, NDC, 16], F32, kind="ExternalInput").ap()
    a2 = nc.dram_tensor("a2", [128, NOC, 16], F32, kind="ExternalInput").ap()
    fcd = nc.dram_tensor("fcd", [NFT, 128, NDC, 128], F32, kind="ExternalInput").ap()
    fsd = nc.dram_tensor("fsd", [NFT, 128, NDC, 128], F32, kind="ExternalInput").ap()
    cid = nc.dram_tensor("cid", [NDC, 128, NFT, 128], F32, kind="ExternalInput").ap()
    sid = nc.dram_tensor("sid", [NDC, 128, NFT, 128], F32, kind="ExternalInput").ap()
    idm = nc.dram_tensor("idm", [128, 128], F32, kind="ExternalInput").ap()

    outd = nc.dram_tensor("out", [DM, NT], F32, kind="ExternalOutput").ap()
    dbg_mode = stop_after == "ffn1"
    if dbg_mode:
        ydbg = nc.dram_tensor("ydbg", [DM, NT], F32, kind="ExternalOutput").ap()
        gdbg = nc.dram_tensor("gdbg", [2048, TH], F32, kind="ExternalOutput").ap()
        tdbg = nc.dram_tensor("tdbg", [128, 3 * TH], F32, kind="ExternalOutput").ap()
        mdbg = nc.dram_tensor("mdbg", [128, TH], F32, kind="ExternalOutput").ap()
    xs = nc.dram_tensor("xs", [DM, NT], F32).ap()
    w13c = nc.dram_tensor("w13c", [NOC, 128, 66, 128], BF16).ap()
    w2c = nc.dram_tensor("w2c", [NOC + 1, 128, 32, 128], BF16).ap()
    fcc = nc.dram_tensor("fcc", [NFT, 128, NDC, 128], BF16).ap()
    fsc = nc.dram_tensor("fsc", [NFT, 128, NDC, 128], BF16).ap()
    cic = nc.dram_tensor("cic", [NDC, 128, NFT, 128], BF16).ap()
    sic = nc.dram_tensor("sic", [NDC, 128, NFT, 128], BF16).ap()
    xre_d = nc.dram_tensor("xre_d", [NFT, 128, NT], BF16).ap()
    xim_d = nc.dram_tensor("xim_d", [NFT, 128, NT], BF16).ap()

    with ExitStack() as ctx:
        E = ctx.enter_context
        ARN = E(nc.sbuf_tensor("ARN", [128, ARN_N], BF16))
        ARNF = E(nc.sbuf_tensor("ARNF", [128, ARNF_N], F32))
        a1s = E(nc.sbuf_tensor("a1s", [128, NDC * 16], BF16))
        a3s = E(nc.sbuf_tensor("a3s", [128, NDC * 16], BF16))
        a2s = E(nc.sbuf_tensor("a2s", [128, NOC * 16], BF16))
        t1p = E(nc.sbuf_tensor("t1p", [128, TH], BF16))
        t3p = E(nc.sbuf_tensor("t3p", [128, TH], BF16))
        t2p = E(nc.sbuf_tensor("t2p", [128, TH], BF16))
        ident = E(nc.sbuf_tensor("ident", [128, 128], BF16))
        onesf = E(nc.sbuf_tensor("onesf", [128, 128], F32))
        strow = E(nc.sbuf_tensor("strow", [1, 1536], F32))
        m8f = E(nc.sbuf_tensor("m8f", [128, 8], F32))
        e8 = E(nc.sbuf_tensor("e8", [128, 8], F32))
        w8 = E(nc.sbuf_tensor("w8", [128, 8], F32))
        sc2 = E(nc.sbuf_tensor("sc2", [128, 2], F32))
        twoc = E(nc.sbuf_tensor("twoc", [128, 1], F32))

        pb = [E(nc.psum_tensor(f"pb{i}", [128, 512], F32)) for i in range(6)]
        ptb = [E(nc.psum_tensor(f"ptb{i}", [128, 1024], BF16)) for i in range(2)]

        sT = Sem(ctx, nc, "sT")
        sV = Sem(ctx, nc, "sV")
        sA = Sem(ctx, nc, "sA")
        sB = Sem(ctx, nc, "sB")

        P = Plan()

        def DS(name):
            return DStream(ctx, nc, name, P)

        ini = DS("ini")
        wbs = [DS("wb0"), DS("wb1")]
        w2s = [DS("w20"), DS("w21")]
        b2s = DS("b2s")
        fcs = [DS("fc0"), DS("fc1")]
        fss = [DS("fs0"), DS("fs1")]
        cis = [DS("ci0"), DS("ci1")]
        sis = [DS("si0"), DS("si1")]
        dbs = DS("dbs")
        GP_STREAMS = [ini, wbs[0], wbs[1], w2s[0], w2s[1], b2s,
                      fcs[0], fcs[1], fss[0], fss[1], cis[0], cis[1],
                      sis[0], sis[1], dbs]
        lns = [DS("ln0"), DS("ln1")]
        uls = DS("uls")
        xhs = DS("xhs")
        vst = [DS("vst0"), DS("vst1")]
        xsp = [DS("xsp0"), DS("xsp1")]
        ost = DS("ost")
        SY_STREAMS = [lns[0], lns[1], uls, xhs, vst[0], vst[1],
                      xsp[0], xsp[1], ost]
        wk13 = [DS("wk13a"), DS("wk13b")]
        wk2 = [DS("wk2a"), DS("wk2b")]
        fck = [DS("fcka"), DS("fckb")]
        fsk = [DS("fska"), DS("fskb")]
        cik = [DS("cika"), DS("cikb")]
        sik = [DS("sika"), DS("sikb")]
        SC_STREAMS = wk13 + wk2 + fck + fsk + cik + sik

        tev = []

        def arn(off, n):
            return ARN[:, off:off + n]

        def arnf(off, n):
            return ARNF[:, off:off + n]

        def barrier():
            P.emit("gpsimd", wait=tuple(s.tok() for s in GP_STREAMS), inc=(sB, 1))
            P.emit("sync", wait=tuple(s.tok() for s in SY_STREAMS), inc=(sB, 1))
            P.emit("tensor", wait=((sT, sT.n),), inc=(sB, 1))
            P.emit("vector", wait=((sV, sV.n),), inc=(sB, 1))
            P.emit("scalar", wait=((sA, sA.n),) + tuple(s_.tok() for s_ in SC_STREAMS),
                   inc=(sB, 1))
            tgt = sB.n
            for eng in ENGINES:
                P.emit(eng, wait=((sB, tgt),))

        def init_phase():
            ini.load("gpsimd", lambda g: g.dma_start(a1s[:], a1[:, :, :]))
            ini.load("gpsimd", lambda g: g.dma_start(a3s[:], a3[:, :, :]))
            ini.load("gpsimd", lambda g: g.dma_start(a2s[:], a2[:, :, :]))
            ini.load("gpsimd", lambda g: g.dma_start(ident[:], idm[:]))
            P.emit("vector", lambda v: v.memset(onesf[:], 1.0), inc=(sV, 1))
            P.emit("vector", lambda v: v.memset(twoc[:], 2.0), inc=(sV, 1))
            P.emit("vector", lambda v: v.memset(t1p[:], 0.0), inc=(sV, 1))
            P.emit("vector", lambda v: v.memset(t3p[:], 0.0), inc=(sV, 1))
            P.emit("vector", lambda v: v.memset(t2p[:], 0.0), inc=(sV, 1))

        def layernorm_half(src, th, ybase, pb45w=()):
            t0 = th * TH
            NG = NDC // 2
            for g_ in range(NG):
                slot = g_ % 2
                xfa = arnf(XF + slot * 1024, 1024)
                ld = lns[slot].load(
                    "sync",
                    (lambda sy, xfa=xfa, g_=g_, t0=t0:
                     sy.dma_start(xfa.rearrange("p (c j) -> p c j", c=2),
                                  src[g_ * 256:(g_ + 1) * 256, t0:t0 + TH]
                                  .rearrange("(c p) j -> p c j", c=2))),
                    wait=((sT, sT.n), (sV, sV.n)))
                for ci in range(2):
                    dc = 2 * g_ + ci
                    xc = arnf(XF + slot * 1024 + ci * 512, 512)
                    sq = arnf(SQF + (dc % 2) * 512, 512)
                    P.emit("scalar",
                           (lambda s, sq=sq, xc=xc: s.activation(sq, xc, AF.Square)),
                           wait=(ld, (sT, sT.n)),
                           inc=(sA, 1))
                    sqdone = sA.n
                    P.emit("tensor",
                           (lambda t, xc=xc, dc=dc:
                            t.matmul(pb[4][0:1, :], onesf[:, 0:1], xc,
                                     start=(dc == 0), stop=(dc == NDC - 1))),
                           wait=(ld,) + ((((sA, sA.n), (sV, sV.n)) + tuple(pb45w)) if dc == 0 else ()))
                    P.emit("tensor",
                           (lambda t, sq=sq, dc=dc:
                            t.matmul(pb[5][0:1, :], onesf[:, 0:1], sq,
                                     start=(dc == 0), stop=(dc == NDC - 1))),
                           wait=((sA, sqdone),),
                           inc=(sT, 1))
            mmdone = sT.n
            mrow = strow[0:1, 0:512]
            srow = strow[0:1, 512:1024]
            brow = strow[0:1, 1024:1536]

            def step(fn, extra=()):
                P.emit("vector", fn, wait=((sV, sV.n),) + extra, inc=(sV, 1))

            step((lambda v: v.tensor_scalar(mrow, pb[4][0:1, :], 1.0 / DM, None, ALU.mult)),
                 extra=((sT, mmdone),))
            step(lambda v: v.tensor_scalar(srow, pb[5][0:1, :], 1.0 / DM, None, ALU.mult))
            step(lambda v: v.tensor_tensor(brow, mrow, mrow, ALU.mult))
            step(lambda v: v.tensor_tensor(srow, srow, brow, ALU.subtract))
            step(lambda v: v.tensor_scalar(srow, srow, 1e-5, None, ALU.add))
            vdone = sV.n
            P.emit("scalar", (lambda s: s.activation(srow, srow, AF.Sqrt)),
                   wait=((sV, vdone),), inc=(sA, 1))
            step((lambda v: v.reciprocal(srow, srow)), extra=((sA, sA.n),))
            step(lambda v: v.tensor_tensor(brow, mrow, srow, ALU.mult))
            step(lambda v: v.tensor_scalar(brow, brow, -1.0, None, ALU.mult))
            rowsdone = sV.n
            P.emit("tensor",
                   (lambda t: t.matmul(pb[4][:, :], onesf[0:1, :], srow, start=True, stop=True)),
                   wait=((sV, rowsdone),))
            P.emit("tensor",
                   (lambda t: t.matmul(pb[5][:, :], onesf[0:1, :], brow, start=True, stop=True)),
                   inc=(sT, 1))
            bcdone = sT.n
            step((lambda v: v.tensor_copy(arnf(ABF, 512), pb[4][:, :])),
                 extra=((sT, bcdone),))
            step(lambda v: v.tensor_copy(arnf(ABF + 512, 512), pb[5][:, :]))
            abdone = sV.n
            for g_ in range(NG):
                slot = g_ % 2
                xfa = arnf(XF + slot * 1024, 1024)
                ld = lns[slot].load(
                    "sync",
                    (lambda sy, xfa=xfa, g_=g_, t0=t0:
                     sy.dma_start(xfa.rearrange("p (c j) -> p c j", c=2),
                                  src[g_ * 256:(g_ + 1) * 256, t0:t0 + TH]
                                  .rearrange("(c p) j -> p c j", c=2))),
                    wait=((sV, sV.n if g_ >= 2 else abdone),))
                for ci in range(2):
                    dc = 2 * g_ + ci
                    xc = arnf(XF + slot * 1024 + ci * 512, 512)
                    tmp = arnf(SQF + (dc % 2) * 512, 512)
                    yc = arn(ybase + dc * 512, 512)
                    step((lambda v, tmp=tmp, xc=xc:
                          v.tensor_tensor(tmp, xc, arnf(ABF, 512), ALU.mult)),
                         extra=(ld,))
                    step(lambda v, yc=yc, tmp=tmp:
                         v.tensor_tensor(yc, tmp, arnf(ABF + 512, 512), ALU.add))
            return sV.n

        def ffn_pass(src, dst, dbg=False, wmodes=("read", "read")):
            m2ev = [0]
            ydones = [None, None]
            ydones[0] = layernorm_half(src, 0, YB)
            for th in range(2):
                wmode = wmodes[th]
                ydone = ydones[th]
                t0 = th * TH
                if dbg:
                    for dc in range(NDC):
                        dbs.load("gpsimd",
                                 (lambda g, dc=dc, t0=t0:
                                  g.dma_start(ydbg[dc * 128:dc * 128 + 128, t0:t0 + TH],
                                              arn(YB + dc * 512, 512))),
                                 wait=((sV, ydone),))
                for dc in range(NDC):
                    yc = arn(YB + dc * 512, 512)
                    P.emit("tensor",
                           (lambda t, dc=dc, yc=yc:
                            t.matmul(pb[4][0:16, :], a1s[:, dc * 16:dc * 16 + 16], yc,
                                     start=(dc == 0), stop=(dc == NDC - 1))),
                           wait=((sV, ydone), ini.tok()))
                    P.emit("tensor",
                           (lambda t, dc=dc, yc=yc:
                            t.matmul(pb[5][0:16, :], a3s[:, dc * 16:dc * 16 + 16], yc,
                                     start=(dc == 0), stop=(dc == NDC - 1))),
                           inc=(sT, 1) if dc == NDC - 1 else None)
                tl = sT.n
                P.emit("scalar",
                       (lambda s: s.activation(t1p[0:16, :], pb[4][0:16, :], AF.Copy,
                                               scale=twoc[0:16, 0:1])),
                       wait=((sT, tl),), inc=(sA, 1))
                P.emit("scalar",
                       (lambda s: s.activation(t3p[0:16, :], pb[5][0:16, :], AF.Copy,
                                               scale=twoc[0:16, 0:1])),
                       inc=(sA, 1))
                tldone = sA.n

                pbw = (sV, m2ev[0]) if th == 1 else (sV, 0)
                bandfree = []
                gvl = []
                base_t = sT.n
                for ot in range(NOC):
                    slot = ot % 2
                    wband = WB + slot * 8448
                    wsrc = w13 if wmode == "write" else w13c
                    wtok = wbs[slot].load(
                        "gpsimd",
                        (lambda g, ot=ot, wband=wband, wsrc=wsrc:
                         g.dma_start(arn(wband, 8448), wsrc[ot, :, :, :])),
                        wait=((sT, bandfree[ot - 2] if ot >= 2 else base_t),
                              wk13[slot].tok()))
                    if wmode == "write":
                        wk13[slot].load(
                            "scalar",
                            (lambda sc, ot=ot, wband=wband:
                             sc.dma_start(w13c[ot, :, :, :], arn(wband, 8448))),
                            wait=(wtok,))
                    bh1 = pb[0 + slot * 2]
                    bh3 = pb[1 + slot * 2]
                    bw = (sV, gvl[ot - 2]) if ot >= 2 else pbw
                    for dc in range(NDC):
                        P.emit("tensor",
                               (lambda t, bh1=bh1, wband=wband, dc=dc:
                                t.matmul(bh1[:, :], arn(wband + dc * 128, 128),
                                         arn(YB + dc * 512, 512),
                                         start=(dc == 0), stop=False)),
                               wait=(wtok,) + ((bw,) if dc == 0 else ()))
                        P.emit("tensor",
                               (lambda t, bh3=bh3, wband=wband, dc=dc:
                                t.matmul(bh3[:, :], arn(wband + (33 + dc) * 128, 128),
                                         arn(YB + dc * 512, 512),
                                         start=(dc == 0), stop=False)))
                    P.emit("tensor",
                           (lambda t, bh1=bh1, wband=wband:
                            t.matmul(bh1[:, :], arn(wband + 32 * 128, 128), t1p[:, :],
                                     start=False, stop=True)),
                           wait=((sA, tldone),))
                    P.emit("tensor",
                           (lambda t, bh3=bh3, wband=wband:
                            t.matmul(bh3[:, :], arn(wband + 65 * 128, 128), t3p[:, :],
                                     start=False, stop=True)),
                           inc=(sT, 1))
                    bandfree.append(sT.n)
                    mmd = sT.n
                    sil = arnf(SIL + slot * 512, 512)
                    P.emit("scalar",
                           (lambda s, sil=sil, bh1=bh1: s.activation(sil, bh1[:, :], AF.Silu)),
                           wait=((sT, mmd),
                                 (sV, gvl[ot - 2]) if ot >= 2 else pbw),
                           inc=(sA, 1))
                    sd = sA.n
                    P.emit("vector",
                           (lambda v, ot=ot, sil=sil, bh3=bh3:
                            v.tensor_tensor(arn(GB + ot * 512, 512), sil, bh3[:, :], ALU.mult)),
                           wait=((sA, sd),), inc=(sV, 1))
                    gvl.append(sV.n)
                gdone = sV.n
                if dbg and th == 0:
                    for i, ot in enumerate(list(range(8)) + list(range(78, 86))):
                        dbs.load("gpsimd",
                                 (lambda g, i=i, ot=ot:
                                  g.dma_start(gdbg[i * 128:i * 128 + 128, :],
                                              arn(GB + ot * 512, 512))),
                                 wait=((sV, gdone),))
                    dbs.load("gpsimd", (lambda g: g.dma_start(tdbg[:, 0:TH], t1p[:, :])),
                             wait=((sA, tldone),))
                    dbs.load("gpsimd", (lambda g: g.dma_start(tdbg[:, TH:2 * TH], t3p[:, :])))

                for oc in range(NOC):
                    P.emit("tensor",
                           (lambda t, oc=oc:
                            t.matmul(pb[4][0:16, :], a2s[:, oc * 16:oc * 16 + 16],
                                     arn(GB + oc * 512, 512),
                                     start=(oc == 0), stop=(oc == NOC - 1))),
                           wait=((sV, gdone),) if oc == 0 else (),
                           inc=(sT, 1) if oc == NOC - 1 else None)
                P.emit("scalar",
                       (lambda s: s.activation(t2p[0:16, :], pb[4][0:16, :], AF.Copy,
                                               scale=twoc[0:16, 0:1])),
                       wait=((sT, sT.n),), inc=(sA, 1))
                t2done = sA.n
                if dbg and th == 0:
                    dbs.load("gpsimd", (lambda g: g.dma_start(tdbg[:, 2 * TH:3 * TH], t2p[:, :])),
                             wait=((sA, t2done),))

                if th == 0:
                    ydones[1] = layernorm_half(src, 1, YB, pb45w=((sA, t2done),))

                prev_ev = gdone
                for rg in range(8):
                    et0 = rg * NETR
                    b2tok = b2s.load(
                        "gpsimd",
                        (lambda g, et0=et0:
                         g.dma_start(arn(B2B, 512).rearrange("p (e j) -> p e j", e=4),
                                     w2[NOC, :, et0:et0 + 4, :])),
                        wait=((sT, sT.n),))
                    utok = uls.load(
                        "sync",
                        (lambda sy, et0=et0, t0=t0:
                         sy.dma_start(arnf(UF, 2048).rearrange("p (e j) -> p e j", e=4),
                                      src[et0 * 128:et0 * 128 + 512, t0:t0 + TH]
                                      .rearrange("(e p) j -> p e j", e=4))),
                        wait=((sV, prev_ev),))
                    w2free = []
                    NGW = (NOC + 3) // 4
                    base_t2 = sT.n
                    for gw in range(NGW):
                        oc0 = gw * 4
                        noc = min(4, NOC - oc0)
                        slot = gw % 2
                        w2src = w2 if wmode == "write" else w2c
                        wtok = w2s[slot].load(
                            "gpsimd",
                            (lambda g, oc0=oc0, noc=noc, slot=slot, et0=et0, w2src=w2src:
                             g.dma_start(
                                 arn(W2B + slot * 2048, noc * 512)
                                 .rearrange("p (o x) -> p o x", o=noc),
                                 w2src[oc0:oc0 + noc, :, et0:et0 + 4, :]
                                 .rearrange("o p e j -> p o (e j)"))),
                            wait=((sT, w2free[gw - 2] if gw >= 2 else base_t2),
                                  wk2[slot].tok()))
                        if wmode == "write":
                            wk2[slot].load(
                                "scalar",
                                (lambda sc, oc0=oc0, noc=noc, slot=slot, et0=et0:
                                 sc.dma_start(
                                     w2c[oc0:oc0 + noc, :, et0:et0 + 4, :]
                                     .rearrange("o p e j -> p o (e j)"),
                                     arn(W2B + slot * 2048, noc * 512)
                                     .rearrange("p (o x) -> p o x", o=noc))),
                                wait=(wtok,))
                        for oi in range(noc):
                            oc = oc0 + oi
                            for ei in range(NETR):
                                last = (oi == noc - 1 and ei == NETR - 1)
                                P.emit("tensor",
                                       (lambda t, slot=slot, oi=oi, ei=ei, oc=oc:
                                        t.matmul(pb[ei][:, :],
                                                 arn(W2B + slot * 2048 + oi * 512 + ei * 128, 128),
                                                 arn(GB + oc * 512, 512),
                                                 start=(oc == 0), stop=False)),
                                       wait=(wtok,) +
                                            (((sV, prev_ev),) if (oc == 0 and ei == 0) else ()),
                                       inc=(sT, 1) if last else None)
                        w2free.append(sT.n)
                    for ei in range(NETR):
                        P.emit("tensor",
                               (lambda t, ei=ei:
                                t.matmul(pb[ei][:, :], arn(B2B + ei * 128, 128), t2p[:, :],
                                         start=False, stop=True)),
                               wait=(b2tok, (sA, t2done)),
                               inc=(sT, 1) if ei == NETR - 1 else None)
                    rgmm = sT.n
                    tapdone = 0
                    if dbg and th == 0 and rg == 0:
                        P.emit("scalar",
                               (lambda s: s.copy(arnf(SIL, 512), pb[0][:, :])),
                               wait=((sT, rgmm),), inc=(sA, 1))
                        tapdone = sA.n
                        dbs.load("gpsimd",
                                 (lambda g: g.dma_start(mdbg[:, :], arnf(SIL, 512))),
                                 wait=((sA, tapdone),))
                    for ei in range(NETR):
                        vslot = arnf(VF + (ei % 2) * 512, 512)
                        P.emit("vector",
                               (lambda v, ei=ei, vslot=vslot:
                                v.scalar_tensor_tensor(vslot, pb[ei][:, :], 0.5,
                                                       arnf(UF + ei * 512, 512),
                                                       ALU.mult, ALU.add)),
                               wait=((sT, rgmm), (sA, tapdone), utok, vst[ei % 2].tok()),
                               inc=(sV, 1))
                        vd = sV.n
                        vst[ei % 2].load(
                            "sync",
                            (lambda sy, ei=ei, et0=et0, vslot=vslot, t0=t0:
                             sy.dma_start(dst[(et0 + ei) * 128:(et0 + ei) * 128 + 128,
                                              t0:t0 + TH], vslot)),
                            wait=((sV, vd),))
                    prev_ev = sV.n
                if th == 0:
                    m2ev[0] = sV.n

        def attn_phase():
            for th in range(2):
                cmode = "write" if th == 0 else "read"
                t0 = th * TH
                ydone = layernorm_half(xs, th, AYB)

                xtf, xev, ppv = [], [], []
                base_t = sT.n
                for ft in range(NFT):
                    slot = ft % 2
                    fcb = XFC + slot * 4096
                    fsb = XFS + slot * 4096
                    fcsrc = fcd if cmode == "write" else fcc
                    fssrc = fsd if cmode == "write" else fsc
                    fctok = fcs[slot].load(
                        "gpsimd",
                        (lambda g, ft=ft, fcb=fcb, fcsrc=fcsrc:
                         g.dma_start(arn(fcb, 4096), fcsrc[ft, :, :, :])),
                        wait=((sT, xtf[ft - 2] if ft >= 2 else base_t),
                              fck[slot].tok()))
                    fstok = fss[slot].load(
                        "gpsimd",
                        (lambda g, ft=ft, fsb=fsb, fssrc=fssrc:
                         g.dma_start(arn(fsb, 4096), fssrc[ft, :, :, :])),
                        wait=(fsk[slot].tok(),))
                    if cmode == "write":
                        fck[slot].load(
                            "scalar",
                            (lambda sc, ft=ft, fcb=fcb:
                             sc.dma_start(fcc[ft, :, :, :], arn(fcb, 4096))),
                            wait=(fctok,))
                        fsk[slot].load(
                            "scalar",
                            (lambda sc, ft=ft, fsb=fsb:
                             sc.dma_start(fsc[ft, :, :, :], arn(fsb, 4096))),
                            wait=(fstok,))
                    bre = pb[0 + 2 * slot]
                    bim = pb[1 + 2 * slot]
                    bw = (sA, xev[ft - 2]) if ft >= 2 else (sV, 0)
                    for dc in range(NDC):
                        P.emit("tensor",
                               (lambda t, bre=bre, fcb=fcb, dc=dc:
                                t.matmul(bre[:, :], arn(fcb + dc * 128, 128),
                                         arn(AYB + dc * 512, 512),
                                         start=(dc == 0), stop=(dc == NDC - 1))),
                               wait=(fctok, (sV, ydone)) + ((bw,) if dc == 0 else ()))
                    for dc in range(NDC):
                        P.emit("tensor",
                               (lambda t, bim=bim, fsb=fsb, dc=dc:
                                t.matmul(bim[:, :], arn(fsb + dc * 128, 128),
                                         arn(AYB + dc * 512, 512),
                                         start=(dc == 0), stop=(dc == NDC - 1))),
                               wait=(fstok,) if dc == 0 else (),
                               inc=(sT, 1) if dc == NDC - 1 else None)
                    xtf.append(sT.n)
                    mmd = sT.n
                    xre_sl = arn(XSL + slot * 1024, 512)
                    xim_sl = arn(XSL + slot * 1024 + 512, 512)
                    tq1 = arnf(SIL, 512)
                    tq2 = arnf(SIL + 512, 512)
                    P.emit("scalar",
                           (lambda s, xre_sl=xre_sl, bre=bre: s.copy(xre_sl, bre[:, :])),
                           wait=((sT, mmd), xsp[slot].tok(),
                                 (sV, ppv[ft - 1] if ft >= 1 else 0)),
                           inc=(sA, 1))
                    P.emit("scalar",
                           (lambda s, xim_sl=xim_sl, bim=bim: s.copy(xim_sl, bim[:, :])),
                           inc=(sA, 1))
                    P.emit("scalar",
                           (lambda s, tq1=tq1, bre=bre: s.activation(tq1, bre[:, :], AF.Square)),
                           inc=(sA, 1))
                    P.emit("scalar",
                           (lambda s, tq2=tq2, bim=bim: s.activation(tq2, bim[:, :], AF.Square)),
                           inc=(sA, 1))
                    xev.append(sA.n)
                    evd = sA.n
                    P.emit("vector",
                           (lambda v, ft=ft, tq1=tq1, tq2=tq2:
                            v.tensor_tensor(arn(PPB + ft * 512, 512), tq1, tq2, ALU.add)),
                           wait=((sA, evd),), inc=(sV, 1))
                    ppv.append(sV.n)
                    xsp[slot].load(
                        "sync",
                        (lambda sy, ft=ft, xre_sl=xre_sl, t0=t0:
                         sy.dma_start(xre_d[ft, :, t0:t0 + TH], xre_sl)),
                        wait=((sA, evd),))
                    xsp[slot].load(
                        "sync",
                        (lambda sy, ft=ft, xim_sl=xim_sl, t0=t0:
                         sy.dma_start(xim_d[ft, :, t0:t0 + TH], xim_sl)))
                barrier()

                ctf, crevl, trlast = [], [], []
                base_t = sT.n
                for lt in range(NDC):
                    slot = lt % 2
                    civ = CIV + slot * 2176
                    cisrc = cid if cmode == "write" else cic
                    citok = cis[slot].load(
                        "gpsimd",
                        (lambda g, lt=lt, civ=civ, cisrc=cisrc:
                         g.dma_start(arn(civ, 2176), cisrc[lt, :, :, :])),
                        wait=((sT, ctf[lt - 2] if lt >= 2 else base_t),
                              cik[slot].tok()))
                    if cmode == "write":
                        cik[slot].load(
                            "scalar",
                            (lambda sc, lt=lt, civ=civ:
                             sc.dma_start(cic[lt, :, :, :], arn(civ, 2176))),
                            wait=(citok,))
                    bco = pb[slot]
                    for kc in range(NFT):
                        P.emit("tensor",
                               (lambda t, bco=bco, civ=civ, kc=kc:
                                t.matmul(bco[:, :], arn(civ + kc * 128, 128),
                                         arn(PPB + kc * 512, 512),
                                         start=(kc == 0), stop=(kc == NFT - 1))),
                               wait=(citok,) +
                                    (((sA, crevl[lt - 2]),) if (kc == 0 and lt >= 2) else ()),
                               inc=(sT, 1) if kc == NFT - 1 else None)
                    ctf.append(sT.n)
                    cmm = sT.n
                    crs = arn(CRR + slot * 512, 512)
                    P.emit("scalar",
                           (lambda s, crs=crs, bco=bco: s.copy(crs, bco[:, :])),
                           wait=((sT, cmm), (sT, trlast[lt - 2] if lt >= 2 else 0)),
                           inc=(sA, 1))
                    crevl.append(sA.n)
                    crd = sA.n
                    for tt in range(4):
                        pt = ptb[len(tev) % 2]
                        P.emit("tensor",
                               (lambda t, pt=pt, crs=crs, tt=tt:
                                t.transpose(pt[:, 0:128], crs[:, tt * 128:tt * 128 + 128],
                                            ident[:])),
                               wait=((sA, crd), (sA, tev[-2] if len(tev) >= 2 else 0)),
                               inc=(sT, 1))
                        td = sT.n
                        P.emit("scalar",
                               (lambda s, tt=tt, lt=lt, pt=pt:
                                s.copy(arn(CRT + tt * 4096 + lt * 128, 128), pt[:, 0:128])),
                               wait=((sT, td),), inc=(sA, 1))
                        tev.append(sA.n)
                    trlast.append(sT.n)
                barrier()

                for tt in range(4):
                    ctile = arn(CRT + tt * 4096, 4096)
                    cf = arnf(CRTF, 4096)

                    def vstep(fn, extra=()):
                        P.emit("vector", fn, wait=((sV, sV.n),) + extra, inc=(sV, 1))

                    vstep((lambda v, ctile=ctile, cf=cf: v.tensor_copy(cf, ctile)),
                          extra=((sT, sT.n), (sA, sA.n)))
                    vstep(lambda v, cf=cf: v.max(out=m8f[:], in_=cf))
                    vstep(lambda v: v.tensor_scalar(sc2[:, 0:1], m8f[:, 0:1], -1.0,
                                                    None, ALU.mult))
                    ng = sV.n
                    P.emit("scalar",
                           (lambda s: s.activation(e8[:], m8f[:], AF.Exp, bias=sc2[:, 0:1])),
                           wait=((sV, ng),), inc=(sA, 1))
                    vstep((lambda v: v.reduce_sum(sc2[:, 1:2], e8[:], axis=AX.X)),
                          extra=((sA, sA.n),))
                    vstep(lambda v: v.reciprocal(sc2[:, 1:2], sc2[:, 1:2]))
                    vstep(lambda v: v.tensor_scalar(w8[:], e8[:], sc2[:, 1:2], None, ALU.mult))
                    hacc = arn(HBB, 4096)
                    hk = arn(HBB + 4096, 4096)
                    vstep(lambda v, cf=cf, hacc=hacc:
                          v.tensor_scalar(hacc, cf, m8f[:, 0:1], w8[:, 0:1],
                                          ALU.is_equal, ALU.mult))
                    for k in range(1, 8):
                        vstep(lambda v, cf=cf, hk=hk, k=k:
                              v.tensor_scalar(hk, cf, m8f[:, k:k + 1], w8[:, k:k + 1],
                                              ALU.is_equal, ALU.mult))
                        vstep(lambda v, hacc=hacc, hk=hk:
                              v.tensor_tensor(hacc, hacc, hk, ALU.add))
                    hd = sV.n
                    for dc in range(NDC):
                        pt = ptb[len(tev) % 2]
                        P.emit("tensor",
                               (lambda t, pt=pt, hacc=hacc, dc=dc:
                                t.transpose(pt[:, 0:128], hacc[:, dc * 128:dc * 128 + 128],
                                            ident[:])),
                               wait=((sV, hd), (sA, tev[-2] if len(tev) >= 2 else 0)),
                               inc=(sT, 1))
                        td = sT.n
                        P.emit("scalar",
                               (lambda s, dc=dc, tt=tt, pt=pt:
                                s.copy(arn(HTB + dc * 512 + tt * 128, 128), pt[:, 0:128])),
                               wait=((sT, td),), inc=(sA, 1))
                        tev.append(sA.n)
                barrier()

                htf, hev, riv = [], [], []
                base_t = sT.n
                for ft in range(NFT):
                    slot = ft % 2
                    fcb = HFC + slot * 4096
                    fsb = HFS + slot * 4096
                    fctok = fcs[slot].load(
                        "gpsimd",
                        (lambda g, ft=ft, fcb=fcb:
                         g.dma_start(arn(fcb, 4096), fcc[ft, :, :, :])),
                        wait=((sT, htf[ft - 2] if ft >= 2 else base_t),))
                    fstok = fss[slot].load(
                        "gpsimd",
                        (lambda g, ft=ft, fsb=fsb:
                         g.dma_start(arn(fsb, 4096), fsc[ft, :, :, :])))
                    xrs = arn(HXS, 512)
                    xis = arn(HXS + 512, 512)
                    xhs.load("sync",
                             (lambda sy, ft=ft, xrs=xrs, t0=t0:
                              sy.dma_start(xrs, xre_d[ft, :, t0:t0 + TH])),
                             wait=((sV, riv[ft - 1] if ft >= 1 else 0),))
                    xhtok = xhs.load(
                        "sync",
                        (lambda sy, ft=ft, xis=xis, t0=t0:
                         sy.dma_start(xis, xim_d[ft, :, t0:t0 + TH])))
                    bre = pb[0 + 2 * slot]
                    bim = pb[1 + 2 * slot]
                    bw = (sA, hev[ft - 2]) if ft >= 2 else (sV, 0)
                    for lc in range(NDC):
                        P.emit("tensor",
                               (lambda t, bre=bre, fcb=fcb, lc=lc:
                                t.matmul(bre[:, :], arn(fcb + lc * 128, 128),
                                         arn(HTB + lc * 512, 512),
                                         start=(lc == 0), stop=(lc == NDC - 1))),
                               wait=(fctok, (sA, tev[-1] if tev else 0)) +
                                    ((bw,) if lc == 0 else ()))
                    for lc in range(NDC):
                        P.emit("tensor",
                               (lambda t, bim=bim, fsb=fsb, lc=lc:
                                t.matmul(bim[:, :], arn(fsb + lc * 128, 128),
                                         arn(HTB + lc * 512, 512),
                                         start=(lc == 0), stop=(lc == NDC - 1))),
                               wait=(fstok,) if lc == 0 else (),
                               inc=(sT, 1) if lc == NDC - 1 else None)
                    htf.append(sT.n)
                    mmd = sT.n
                    hre = arn(HXS + 1024, 512)
                    him = arn(HXS + 1536, 512)
                    P.emit("scalar", (lambda s, hre=hre, bre=bre: s.copy(hre, bre[:, :])),
                           wait=((sT, mmd), (sV, riv[ft - 1] if ft >= 1 else 0)),
                           inc=(sA, 1))
                    P.emit("scalar", (lambda s, him=him, bim=bim: s.copy(him, bim[:, :])),
                           inc=(sA, 1))
                    hevd = sA.n
                    hev.append(hevd)
                    tm = [arn(HXS + 2048 + i * 512, 512) for i in range(4)]

                    def vstep2(fn, extra=()):
                        P.emit("vector", fn, wait=((sV, sV.n),) + extra, inc=(sV, 1))

                    vstep2((lambda v, tm=tm, xrs=xrs, hre=hre:
                            v.tensor_tensor(tm[0], xrs, hre, ALU.mult)),
                           extra=((sA, hevd), xhtok))
                    vstep2(lambda v, tm=tm, xis=xis, him=him:
                           v.tensor_tensor(tm[1], xis, him, ALU.mult))
                    vstep2(lambda v, tm=tm, xis=xis, hre=hre:
                           v.tensor_tensor(tm[2], xis, hre, ALU.mult))
                    vstep2(lambda v, tm=tm, xrs=xrs, him=him:
                           v.tensor_tensor(tm[3], xrs, him, ALU.mult))
                    vstep2(lambda v, tm=tm, ft=ft:
                           v.tensor_tensor(arn(RPB + ft * 512, 512), tm[0], tm[1], ALU.add))
                    vstep2(lambda v, tm=tm, ft=ft:
                           v.tensor_tensor(arn(IPB + ft * 512, 512), tm[2], tm[3], ALU.subtract))
                    riv.append(sV.n)
                barrier()

                otf, ovv = [], []
                base_t = sT.n
                for dt in range(NDC):
                    slot = dt % 2
                    civ = OCV + slot * 2176
                    siv = OSV + slot * 2176
                    citok = cis[slot].load(
                        "gpsimd",
                        (lambda g, dt=dt, civ=civ:
                         g.dma_start(arn(civ, 2176), cic[dt, :, :, :])),
                        wait=((sT, otf[dt - 2] if dt >= 2 else base_t),))
                    sisrc = sid if cmode == "write" else sic
                    sitok = sis[slot].load(
                        "gpsimd",
                        (lambda g, dt=dt, siv=siv, sisrc=sisrc:
                         g.dma_start(arn(siv, 2176), sisrc[dt, :, :, :])),
                        wait=(sik[slot].tok(),))
                    if cmode == "write":
                        sik[slot].load(
                            "scalar",
                            (lambda sc, dt=dt, siv=siv:
                             sc.dma_start(sic[dt, :, :, :], arn(siv, 2176))),
                            wait=(sitok,))
                    utok = uls.load(
                        "sync",
                        (lambda sy, dt=dt, t0=t0:
                         sy.dma_start(arnf(XF, 512),
                                      xs[dt * 128:dt * 128 + 128, t0:t0 + TH])),
                        wait=((sV, ovv[dt - 1] if dt >= 1 else 0),))
                    bo = pb[slot]
                    for kc in range(NFT):
                        P.emit("tensor",
                               (lambda t, bo=bo, civ=civ, kc=kc:
                                t.matmul(bo[:, :], arn(civ + kc * 128, 128),
                                         arn(RPB + kc * 512, 512),
                                         start=(kc == 0), stop=False)),
                               wait=(citok,) +
                                    (((sV, ovv[dt - 2]),) if (kc == 0 and dt >= 2) else ()))
                    for kc in range(NFT):
                        P.emit("tensor",
                               (lambda t, bo=bo, siv=siv, kc=kc:
                                t.matmul(bo[:, :], arn(siv + kc * 128, 128),
                                         arn(IPB + kc * 512, 512),
                                         start=False, stop=(kc == NFT - 1))),
                               wait=(sitok,) if kc == 0 else (),
                               inc=(sT, 1) if kc == NFT - 1 else None)
                    otf.append(sT.n)
                    omm = sT.n
                    vslot = arnf(VF + (dt % 2) * 512, 512)
                    P.emit("vector",
                           (lambda v, vslot=vslot, bo=bo:
                            v.scalar_tensor_tensor(vslot, bo[:, :], 1.0, arnf(XF, 512),
                                                   ALU.mult, ALU.add)),
                           wait=((sT, omm), utok, vst[dt % 2].tok()),
                           inc=(sV, 1))
                    ovv.append(sV.n)
                    vd = sV.n
                    vst[dt % 2].load(
                        "sync",
                        (lambda sy, dt=dt, vslot=vslot, t0=t0:
                         sy.dma_start(xs[dt * 128:dt * 128 + 128, t0:t0 + TH], vslot)),
                        wait=((sV, vd),))
                barrier()

        def finish_wait():
            P.emit("sync", wait=tuple(s.tok() for s in SY_STREAMS))

        init_phase()
        barrier()
        if stop_after == "ffn1":
            ffn_pass(xin, outd, dbg=True, wmodes=("write", "read"))
            barrier()
            finish_wait()
        else:
            ffn_pass(xin, xs, wmodes=("write", "read"))
            barrier()
            attn_phase()
            if stop_after == "attn":
                for dc in range(8):
                    ost.load("sync",
                             (lambda sy, dc=dc:
                              sy.dma_start(outd[dc * 512:(dc + 1) * 512, :],
                                           xs[dc * 512:(dc + 1) * 512, :])))
                finish_wait()
            else:
                ffn_pass(xs, outd, wmodes=("read", "read"))
                barrier()
                finish_wait()

        with nc.Block() as block:
            @block.tensor
            def _(e):
                for f in P.q["tensor"]:
                    f(e)

            @block.vector
            def _(e):
                for f in P.q["vector"]:
                    f(e)

            @block.scalar
            def _(e):
                for f in P.q["scalar"]:
                    f(e)

            @block.sync
            def _(e):
                for f in P.q["sync"]:
                    f(e)

            @block.gpsimd
            def _(e):
                for f in P.q["gpsimd"]:
                    f(e)

    return nc


_cached = {}


def _prep_host(L0_ffn_w1, L0_ffn_w3, L0_ffn_w2,
               L0_ffn_lora_A1, L0_ffn_lora_B1, L0_ffn_lora_A3, L0_ffn_lora_B3,
               L0_ffn_lora_A2, L0_ffn_lora_B2):
    def tile_w13_half(W, B):
        WT = np.ascontiguousarray(np.asarray(W, dtype=np.float32).T)
        blk = WT.reshape(NDC, 128, NOC, 128).transpose(2, 1, 0, 3)
        BT = np.ascontiguousarray(np.asarray(B, dtype=np.float32).T)
        bb = BT.reshape(16, NOC, 128).transpose(1, 0, 2)
        out = np.zeros((NOC, 128, 33, 128), dtype=np.float32)
        out[:, :, 0:32, :] = blk
        out[:, 0:16, 32, :] = bb
        return out

    b1 = tile_w13_half(L0_ffn_w1, L0_ffn_lora_B1)
    b3 = tile_w13_half(L0_ffn_w3, L0_ffn_lora_B3)
    w13 = np.ascontiguousarray(np.concatenate([b1, b3], axis=2))

    W2T = np.ascontiguousarray(np.asarray(L0_ffn_w2, dtype=np.float32).T)
    w2 = np.zeros((NOC + 1, 128, 32, 128), dtype=np.float32)
    w2[0:NOC] = W2T.reshape(NOC, 128, 32, 128)
    B2T = np.ascontiguousarray(np.asarray(L0_ffn_lora_B2, dtype=np.float32).T)
    w2[NOC, 0:16, :, :] = B2T.reshape(16, 32, 128)
    w2 = np.ascontiguousarray(w2)

    def tile_a(A, nb):
        AT = np.ascontiguousarray(np.asarray(A, dtype=np.float32).T)
        return np.ascontiguousarray(AT.reshape(nb, 128, 16).transpose(1, 0, 2))

    a1 = tile_a(L0_ffn_lora_A1, NDC)
    a3 = tile_a(L0_ffn_lora_A3, NDC)
    a2 = tile_a(L0_ffn_lora_A2, NOC)

    L = DM
    d = np.arange(L, dtype=np.int64)
    k = np.arange(FPAD, dtype=np.int64)
    ang = (2.0 * np.pi / L) * ((d[:, None] * k[None, :]) % L).astype(np.float64)
    fc = np.cos(ang)
    fsn = -np.sin(ang)
    fc[:, 2049:] = 0.0
    fsn[:, 2049:] = 0.0
    fcdh = np.ascontiguousarray(
        fc.astype(np.float32).reshape(NDC, 128, NFT, 128).transpose(2, 1, 0, 3))
    fsdh = np.ascontiguousarray(
        fsn.astype(np.float32).reshape(NDC, 128, NFT, 128).transpose(2, 1, 0, 3))

    alpha = np.zeros(FPAD)
    alpha[0] = 1.0
    alpha[1:2048] = 2.0
    alpha[2048] = 1.0
    angi = (2.0 * np.pi / L) * ((k[:, None] * d[None, :]) % L).astype(np.float64)
    ci = (alpha[:, None] / L) * np.cos(angi)
    si = -(alpha[:, None] / L) * np.sin(angi)
    ci[2049:, :] = 0.0
    si[2049:, :] = 0.0
    cidh = np.ascontiguousarray(
        ci.astype(np.float32).reshape(NFT, 128, NDC, 128).transpose(2, 1, 0, 3))
    sidh = np.ascontiguousarray(
        si.astype(np.float32).reshape(NFT, 128, NDC, 128).transpose(2, 1, 0, 3))

    idm = np.ascontiguousarray(np.eye(128, dtype=np.float32))
    return dict(w13=w13, w2=w2, a1=a1, a3=a3, a2=a2,
                fcd=fcdh, fsd=fsdh, cid=cidh, sid=sidh, idm=idm)


def kernel(L0_attn_x, L0_ffn_w1, L0_ffn_w3, L0_ffn_w2,
           L0_ffn_lora_A1, L0_ffn_lora_B1, L0_ffn_lora_A3, L0_ffn_lora_B3,
           L0_ffn_lora_A2, L0_ffn_lora_B2, _stop_after=None, _cores=None):
    x = np.asarray(L0_attn_x, dtype=np.float32).reshape(NCORES * NT, DM)
    shared = _prep_host(L0_ffn_w1, L0_ffn_w3, L0_ffn_w2,
                        L0_ffn_lora_A1, L0_ffn_lora_B1, L0_ffn_lora_A3,
                        L0_ffn_lora_B3, L0_ffn_lora_A2, L0_ffn_lora_B2)

    if _stop_after not in _cached:
        _cached[_stop_after] = build_nc(stop_after=_stop_after)
    nc = _cached[_stop_after]

    cores = list(range(NCORES)) if _cores is None else _cores
    in_maps = []
    for c in cores:
        xT = np.ascontiguousarray(x[c * NT:(c + 1) * NT, :].T)
        m = dict(shared)
        m["xin"] = xT
        in_maps.append(m)

    res = run_bass_kernel_spmd(nc, in_maps, core_ids=list(range(len(cores))))
    if len(cores) < NCORES:
        return [{k: np.asarray(v) for k, v in r.items()} for r in res.results]
    outs = [np.asarray(r["out"]) for r in res.results]
    full = np.concatenate([o.T for o in outs], axis=0)
    return np.ascontiguousarray(full.reshape(2, 4096, DM))
